# version 02
# baseline (speedup 1.0000x reference)
import numpy as np
from contextlib import ExitStack

import concourse.bass as bass
import concourse.mybir as mybir
from concourse.bass_utils import run_bass_kernel_spmd
from concourse.alu_op_type import AluOpType as OP
from concourse.aot_env import allow_ici_all_to_all

F32 = mybir.dt.float32
BF16 = mybir.dt.bfloat16
AF = mybir.ActivationFunctionType
AX = mybir.AxisListType

P = 128
D = 1024
DC = 8
T = 2048
NT = 16
TH = 2176
NTH = 17
I0 = 2816
NI0 = 22
FM = 2048
FFN = 1024
NE = 9
EPS = 1e-5

STAGE = 6
DEBUG = ()
RG = [[0, 1], [2, 3], [4, 5], [6, 7]]


def build(stage=STAGE, debug=None):
    debug = DEBUG if debug is None else debug
    nc = bass.Bass("TRN2", target_bir_lowering=False)
    dt = nc.dram_tensor
    xin = dt("xin", [TH, D], F32, kind="ExternalInput").ap()
    wg = dt("wg", [I0, D], F32, kind="ExternalInput").ap()
    wu = dt("wu", [I0, D], F32, kind="ExternalInput").ap()
    wdn = dt("wdn", [D, I0], F32, kind="ExternalInput").ap()
    wcT = dt("wcT", [D, 32], F32, kind="ExternalInput").ap()
    bconv = dt("bconv", [32, 1], F32, kind="ExternalInput").ap()
    bcr = dt("bcr", [1, 32], F32, kind="ExternalInput").ap()
    wqT = dt("wqT", [D, 512], F32, kind="ExternalInput").ap()
    wkT = dt("wkT", [D, 512], F32, kind="ExternalInput").ap()
    wvT = dt("wvT", [D, 512], F32, kind="ExternalInput").ap()
    bqd = dt("bqd", [P, 4], F32, kind="ExternalInput").ap()
    bkd = dt("bkd", [P, 4], F32, kind="ExternalInput").ap()
    bvd = dt("bvd", [P, 4], F32, kind="ExternalInput").ap()
    wpT = dt("wpT", [512, D], F32, kind="ExternalInput").ap()
    bpd = dt("bpd", [P, DC], F32, kind="ExternalInput").ap()
    gwd = dt("gwd", [D, 8], F32, kind="ExternalInput").ap()
    gbd = dt("gbd", [1, 8], F32, kind="ExternalInput").ap()
    w1d = dt("w1d", [NE, D, FM], F32, kind="ExternalInput").ap()
    w2d = dt("w2d", [NE, FFN, D], F32, kind="ExternalInput").ap()
    b1Td = dt("b1Td", [FM, NE], F32, kind="ExternalInput").ap()
    b2d = dt("b2d", [NE, D], F32, kind="ExternalInput").ap()
    idfd = dt("idfd", [P, P], F32, kind="ExternalInput").ap()
    idbd = dt("idbd", [P, P], BF16, kind="ExternalInput").ap()
    shmd = dt("shmd", [P, 160], F32, kind="ExternalInput").ap()
    amd = dt("amd", [4, P, 512], BF16, kind="ExternalInput").ap()
    xout = dt("xout", [T, D], F32, kind="ExternalOutput").ap()

    qwgT = dt("qwgT", [D, I0], BF16).ap()
    qwuT = dt("qwuT", [D, I0], BF16).ap()
    qwdT = dt("qwdT", [I0, D], BF16).ap()
    logd = dt("logd", [TH, 32], F32).ap()
    ccxi = dt("ccxi", [D, T], BF16).ap()
    ccxo = dt("ccxo", [2, D, T], BF16).ap()
    ymy = dt("ymy", [512, 2 * T], BF16).ap()
    cri = dt("cri", [2, D, T], F32).ap()
    cro = dt("cro", [D, T], F32).ap()

    stk = ExitStack()
    with stk:
        tk = stk.enter_context(nc.semaphore("tk"))
        vg = stk.enter_context(nc.semaphore("vg"))
        sg = stk.enter_context(nc.semaphore("sg"))
        ccs = stk.enter_context(nc.semaphore("ccs"))
        NSLOT = 90
        dslot = [stk.enter_context(nc.semaphore(f"d{i}")) for i in range(NSLOT)]
        dsval = [0] * NSLOT
        dmap = {}
        st = {"n": 0, "cc": 0}

        uid = [0]

        def _un(name):
            uid[0] += 1
            return f"{name}_{uid[0]}"

        def sbuf_t(name, shape, dtp):
            return nc.sbuf_tensor(_un(name), shape, dtp)

        def psum_t(name, shape, dtp):
            return nc.psum_tensor(_un(name), shape, dtp)

        def sbt(name, shape, dtp):
            return stk.enter_context(sbuf_t(name, shape, dtp))

        stream = sbt("stream", [P, DC, T], F32)
        shalo = sbt("shalo", [P, DC, P], F32)
        idf = sbt("idf_s", [P, P], F32)
        idb = sbt("idb_s", [P, P], BF16)
        onesb = sbt("onesb", [1, P], BF16)
        onesf = sbt("onesf", [1, P], F32)
        mvall = sbt("mvall", [P, NTH, 2], F32)
        rsqall = sbt("rsqall", [P, NTH], F32)
        mall = sbt("mall", [P, NTH], F32)
        sqall = sbt("sqall", [P, NTH], F32)
        neg4 = sbt("neg4", [P, 1], F32)
        rscr = sbt("rscr", [P, 512], F32)
        rdum = sbt("rdum", [P, 512], F32)
        rdum2 = sbt("rdum2", [P, 64], F32)
        vtmp = sbt("vtmp", [P, NTH], F32)

        with nc.Block() as blk:

            def step(engname, fn, waits=()):
                n = st["n"]
                def body(eng, _n=n, _w=tuple(waits)):
                    eng.wait_ge(tk, _n)
                    for (h, v) in _w:
                        eng.wait_ge(h, v)
                    ins = fn(eng)
                    ins.then_inc(tk, 1)
                getattr(blk, engname)(body)
                st["n"] = n + 1

            def dma(engname, out, in_, after=None, waits=(), slow_ok=False):
                if after is None:
                    after = st["n"]
                name = out.tensor.name if hasattr(out, "tensor") else str(out)
                if name not in dmap:
                    dmap[name] = len(dmap) % NSLOT
                sl = dmap[name]
                s = dslot[sl]
                dsval[sl] += 16
                v = dsval[sl]
                def body(eng, _a=after, _w=tuple(waits)):
                    if _a is not None:
                        eng.wait_ge(tk, _a)
                    for (h, vv) in _w:
                        eng.wait_ge(h, vv)
                    eng.dma_start(out=out, in_=in_,
                                  allow_slow_non_contiguous=slow_ok).then_inc(s, 16)
                getattr(blk, engname)(body)
                return (s, v)

            def tick():
                return st["n"]

            def vsync(v, ins):
                ins.then_inc(vg, 1)
                st["vg"] = st.get("vg", 0) + 1
                v.wait_ge(vg, st["vg"])

            def ssync(s, ins):
                ins.then_inc(sg, 1)
                st["sg"] = st.get("sg", 0) + 1
                s.wait_ge(sg, st["sg"])

            def vburn(v, n=2):
                ins = None
                for _ in range(n):
                    ins = v.tensor_copy(rdum2[:, 0:32], rdum[0:P, 0:32])
                return ins

            def vrecip(v, out, in_):
                sh = out.shape
                vsync(v, v.reciprocal(out, in_))
                return v.tensor_copy(rdum[0:sh[0], 0:sh[1]], out)

            def settle():
                def f_v(v):
                    ins = None
                    for _ in range(24):
                        ins = v.tensor_copy(rdum2[:, 0:32], rdum[0:P, 0:32])
                    return ins
                step("vector", f_v)
                def f_s(s):
                    ins = None
                    for _ in range(8):
                        ins = s.activation(rdum2[:, 0:32], rdum[0:P, 0:32], AF.Copy)
                    return ins
                step("scalar", f_s)

            def q_v1(v, src, W, z, mean_ap=None, pre6=False):
                nb = W // 16
                if mean_ap is not None:
                    vsync(v, v.tensor_scalar(src, src, mean_ap, 36.0, op0=OP.subtract, op1=OP.mult))
                elif not pre6:
                    vsync(v, v.tensor_scalar(src, src, 36.0, None, op0=OP.mult))
                vsync(v, v.tensor_reduce(z["bm"][:, 0:nb], src.rearrange("p (b k) -> p b k", k=16),
                                         axis=AX.X, op=OP.max, apply_absolute_value=True))
                vsync(v, v.tensor_scalar(z["s6"][:, 0:nb], z["bm"][:, 0:nb], 1.0 / 216.0, 1e-12,
                                op0=OP.mult, op1=OP.max))
                vsync(v, v.reciprocal(z["r6"][:, 0:nb], z["bm"][:, 0:nb]))
                vsync(v, v.tensor_scalar(z["r6b"][:, 0:nb], z["r6"][:, 0:nb], 5.99994, 1.0 / 3.6e-11,
                                op0=OP.mult, op1=OP.min))
                bc = z["r6b"][:, 0:nb].unsqueeze(2).broadcast_to([P, nb, 16])
                vsync(v, v.tensor_tensor(z["u"][:, 0:W].rearrange("p (b k) -> p b k", k=16),
                                         src.rearrange("p (b k) -> p b k", k=16), bc, op=OP.mult))
                vsync(v, v.tensor_scalar(z["y2"][:, 0:W], z["u"][:, 0:W], 2.0, -2.0, op0=OP.min, op1=OP.max))
                vsync(v, v.tensor_scalar(z["y4"][:, 0:W], z["u"][:, 0:W], 4.0, -4.0, op0=OP.min, op1=OP.max))
                return v.tensor_copy(rdum2[:, 0:4], z["y4"][:, 0:4])

            def q_s(s, W, z):
                s.activation(z["a"][:, 0:W], z["y2"][:, 0:W], AF.Copy, bias=96.0)
                s.activation(z["c"][:, 0:W], z["y2"][:, 0:W], AF.Copy, bias=192.0)
                s.activation(z["b"][:, 0:W], z["y4"][:, 0:W], AF.Copy, bias=192.0)
                s.activation(z["d"][:, 0:W], z["u"][:, 0:W], AF.Copy, bias=192.0, scale=0.5)
                return s.activation(z["e"][:, 0:W], z["y4"][:, 0:W], AF.Copy, bias=192.0, scale=0.5)

            def q_v2(v, qv, W, z):
                nb = W // 16
                t2 = z["y2"]
                t3 = z["y4"]
                vsync(v, v.scalar_tensor_tensor(t2[:, 0:W], z["a"][:, 0:W], -96.0, z["b"][:, 0:W], op0=OP.add, op1=OP.add))
                vsync(v, v.scalar_tensor_tensor(t3[:, 0:W], z["d"][:, 0:W], 2.0, t2[:, 0:W], op0=OP.mult, op1=OP.add))
                vsync(v, v.tensor_tensor(t2[:, 0:W], t3[:, 0:W], z["c"][:, 0:W], op=OP.subtract))
                vsync(v, v.scalar_tensor_tensor(t3[:, 0:W], z["e"][:, 0:W], -2.0, t2[:, 0:W], op0=OP.mult, op1=OP.add))
                bc = z["s6"][:, 0:nb].unsqueeze(2).broadcast_to([P, nb, 16])
                vsync(v, v.tensor_tensor(qv[:, 0:W].rearrange("p (b k) -> p b k", k=16),
                                         t3[:, 0:W].rearrange("p (b k) -> p b k", k=16), bc, op=OP.mult))
                return v.tensor_copy(rdum2[:, 0:4], qv[:, 0:4])

            def mk_qscratch(ph, pref, W):
                z = {}
                for n in ("u", "y2", "y4"):
                    z[n] = ph.enter_context(sbuf_t(pref + n, [P, W], F32))
                for n in ("a", "b", "c", "d", "e"):
                    z[n] = ph.enter_context(sbuf_t(pref + n, [P, W], BF16))
                z["bm"] = ph.enter_context(sbuf_t(pref + "bm", [P, W // 16], F32))
                z["r6"] = ph.enter_context(sbuf_t(pref + "r6", [P, W // 16], F32))
                z["r6b"] = ph.enter_context(sbuf_t(pref + "r6b", [P, W // 16], F32))
                z["s6"] = ph.enter_context(sbuf_t(pref + "s6", [P, W // 16], F32))
                return z

            def quant_tile(src, qv, W, z, mean_ap=None, pre6=False, extra_waits=()):
                step("vector", lambda v: q_v1(v, src, W, z, mean_ap, pre6), waits=extra_waits)
                step("scalar", lambda s: q_s(s, W, z))
                step("vector", lambda v: q_v2(v, qv, W, z))

            def ph_consts():
                w1 = dma("sync", idf[:], idfd)
                w2 = dma("sync", idb[:], idbd)
                def f(g):
                    g.memset(onesb[:], 1.0)
                    g.memset(neg4[:], -4.0)
                    return g.memset(onesf[:], 1.0)
                step("gpsimd", f, waits=(w1, w2))

            def ph_wprep(ph):
                tiles = []
                for (src, dst, nr, W) in ((wg, qwgT, NI0, D), (wu, qwuT, NI0, D),
                                          (wdn, qwdT, DC, I0)):
                    for r in range(nr):
                        tiles.append((src, dst, r, W))
                wld = [ph.enter_context(sbuf_t(f"w_ld{i}", [P, I0], F32)) for i in range(2)]
                qv = ph.enter_context(sbuf_t("w_qv", [P, I0], BF16))
                ev = [ph.enter_context(sbuf_t(f"w_ev{i}", [P, P], BF16)) for i in range(2)]
                pst = [ph.enter_context(psum_t(f"w_ps{i}", [P, P], BF16)) for i in range(2)]
                z = mk_qscratch(ph, "w_", I0)
                free_t = [0, 0]
                last_store = None
                evst = [None, None]
                for i, (src, dst, r, W) in enumerate(tiles):
                    ld = dma("sync", wld[i % 2][:, 0:W], src[r * P:(r + 1) * P, :],
                             after=free_t[i % 2])
                    quant_tile(wld[i % 2][:, 0:W], qv, W, z, extra_waits=(ld,))
                    free_t[i % 2] = tick()
                    for j in range(W // P):
                        jb = j % 2
                        step("tensor", lambda te, _j=j, _b=jb: te.transpose(
                            pst[_b][:], qv[:, _j * P:(_j + 1) * P], idb[:]))
                        step("vector", lambda v, _b=jb: v.tensor_copy(ev[_b][:], pst[_b][:]),
                             waits=(evst[jb],) if evst[jb] else ())
                        last_store = dma("scalar", dst[j * P:(j + 1) * P, r * P:(r + 1) * P],
                                         ev[jb][:], after=tick())
                        evst[jb] = last_store
                step("sync", lambda q: q.nop(), waits=(last_store,))

            def ph_lnstats(ph, ntiles, src_mode, init_stream=False):
                xld = [ph.enter_context(sbuf_t(f"ln_x{i}", [P, D], F32)) for i in range(2)]
                pstr = ph.enter_context(psum_t("ln_p", [P, D], F32))
                bnb = ph.enter_context(sbuf_t("ln_bn", [P, 6 * DC], F32))
                free_t = [0, 0]
                for i in range(ntiles):
                    if src_mode == "dram":
                        ld = dma("sync", xld[i % 2][:], xin[i * P:(i + 1) * P, :],
                                 after=free_t[i % 2])
                        if init_stream:
                            def f_t(te, _b=i % 2):
                                for dc in range(DC):
                                    ins = te.transpose(pstr[:, dc * P:(dc + 1) * P],
                                                       xld[_b][:, dc * P:(dc + 1) * P], idf[:])
                                return ins
                            step("tensor", f_t, waits=(ld,))
                            def f_ev(v, _i=i):
                                dst = stream[:, :, _i * P:(_i + 1) * P] if _i < NT else shalo[:, :, :]
                                return v.tensor_copy(dst, pstr[:].rearrange("p (dc t) -> p dc t", t=P))
                            step("vector", f_ev)
                        wv = (ld,)
                    else:
                        def f_t(te, _i=i):
                            for dc in range(DC):
                                src = stream[:, dc, _i * P:(_i + 1) * P] if _i < NT else shalo[:, dc, :]
                                ins = te.transpose(pstr[:, dc * P:(dc + 1) * P], src, idf[:])
                            return ins
                        step("tensor", f_t)
                        step("vector", lambda v, _b=i % 2: v.tensor_copy(xld[_b][:], pstr[:]))
                        wv = ()
                    def f_bn(v, _b=i % 2, _i=i):
                        ins = None
                        for dc in range(DC):
                            ins = v.bn_stats(bnb[:, dc * 6:(dc + 1) * 6], xld[_b][:, dc * P:(dc + 1) * P])
                        vsync(v, ins)
                        vsync(v, v.bn_aggr(mvall[:, _i, :], bnb[:]))
                        return v.tensor_copy(rdum2[:, 0:2], mvall[:, _i, :])
                    step("vector", f_bn, waits=wv)
                    free_t[i % 2] = tick()
                def f_fin(v):
                    v.tensor_copy(mall[:, 0:ntiles], mvall[:, 0:ntiles, 0])
                    return v.tensor_scalar(vtmp[:, 0:ntiles], mvall[:, 0:ntiles, 1], EPS, None, op0=OP.add)
                step("vector", f_fin)
                step("scalar", lambda s: s.activation(sqall[:, 0:ntiles], vtmp[:, 0:ntiles], AF.Sqrt))
                step("vector", lambda v: vrecip(v, rsqall[:, 0:ntiles], sqall[:, 0:ntiles]))

            def ph_l0ffn(call, dbg=False):
                ntiles = NTH if call == 1 else NT
                with ExitStack() as ph:
                    ph_lnstats(ph, ntiles, "dram" if call == 1 else "stream",
                               init_stream=(call == 1))

                chunks = [(0, 4), (4, 4), (8, 4), (12, 4)]
                if call == 1:
                    chunks.append((16, 1))
                ics = [(0, 512), (512, 512), (1024, 512), (1536, 512), (2048, 512), (2560, 256)]

                with ExitStack() as ph:
                    xld2 = ph.enter_context(sbuf_t("f_x", [P, D], F32))
                    zx = mk_qscratch(ph, "fx_", D)
                    zh = mk_qscratch(ph, "fh_", 512)
                    qxb = ph.enter_context(sbuf_t("f_qx", [P, D], BF16))
                    qxT = ph.enter_context(sbuf_t("f_qxT", [P, DC, 512], BF16))
                    gsc = ph.enter_context(sbuf_t("f_gsc", [P, 512], BF16))
                    gsl = ph.enter_context(sbuf_t("f_gsl", [P, 512], BF16))
                    ysc = ph.enter_context(sbuf_t("f_ysc", [P, 512], BF16))
                    hbuf = ph.enter_context(sbuf_t("f_h", [P, 512], F32))
                    hqb = ph.enter_context(sbuf_t("f_hq", [P, 512], BF16))
                    hqT = ph.enter_context(sbuf_t("f_hqT", [P, NI0, 512], BF16))
                    wgs = [ph.enter_context(sbuf_t(f"f_wg{i}", [P, DC, 512], BF16)) for i in range(2)]
                    wus = [ph.enter_context(sbuf_t(f"f_wu{i}", [P, DC, 512], BF16)) for i in range(2)]
                    wdd = [ph.enter_context(sbuf_t(f"f_wd{i}", [P, NI0, P], BF16)) for i in range(2)]
                    pg = ph.enter_context(psum_t("f_pg", [P, 512], F32))
                    py = ph.enter_context(psum_t("f_py", [P, 512], F32))
                    pd = ph.enter_context(psum_t("f_pd", [P, 512], F32))
                    ptb = ph.enter_context(psum_t("f_ptb", [P, D], BF16))
                    ptf = ph.enter_context(psum_t("f_ptf", [P, D], F32))

                    t_start = tick()
                    wfree = {"g": [t_start, t_start], "u": [t_start, t_start],
                             "d": [t_start, t_start]}
                    kslice = 0
                    kdn = 0
                    for ci, (t0, ntile) in enumerate(chunks):
                        for jj in range(ntile):
                            i = t0 + jj
                            if call == 1:
                                ld = dma("sync", xld2[:], xin[i * P:(i + 1) * P, :], after=tick())
                                quant_tile(xld2[:], qxb, D, zx, mean_ap=mall[:, i:i + 1],
                                           extra_waits=(ld,))
                            else:
                                def f_t(te, _i=i):
                                    for dc in range(DC):
                                        ins = te.transpose(ptf[:, dc * P:(dc + 1) * P],
                                                           stream[:, dc, _i * P:(_i + 1) * P], idf[:])
                                    return ins
                                step("tensor", f_t)
                                step("vector", lambda v: v.tensor_copy(xld2[:], ptf[:]))
                                quant_tile(xld2[:], qxb, D, zx, mean_ap=mall[:, i:i + 1])
                            if dbg and ci == 0 and jj == 0:
                                dqx = dt("dbg_qx", [P, D], BF16, kind="ExternalOutput").ap()
                                s_ = dma("scalar", dqx, qxb[:], after=tick())
                                step("sync", lambda q, _s=s_: q.nop(), waits=(s_,))
                            def f_tq(te):
                                for dc in range(DC):
                                    ins = te.transpose(ptb[:, dc * P:(dc + 1) * P],
                                                       qxb[:, dc * P:(dc + 1) * P], idb[:])
                                return ins
                            step("tensor", f_tq)
                            step("vector", lambda v, _jj=jj: v.tensor_copy(
                                qxT[:, :, _jj * P:(_jj + 1) * P],
                                ptb[:].rearrange("p (dc t) -> p dc t", t=P)))

                        for ic, (ic0, icw) in enumerate(ics):
                            kb = kslice % 2
                            lg = dma("sync", wgs[kb][:, :, 0:icw],
                                     qwgT.rearrange("(dc p) i -> p dc i", p=P)[:, :, ic0:ic0 + icw],
                                     after=wfree["g"][kb])
                            lu = dma("sync", wus[kb][:, :, 0:icw],
                                     qwuT.rearrange("(dc p) i -> p dc i", p=P)[:, :, ic0:ic0 + icw],
                                     after=wfree["u"][kb])
                            kslice += 1
                            for jj in range(ntile):
                                i = t0 + jj
                                def f_mm(te, _jj=jj, _kb=kb, _icw=icw):
                                    for dc in range(DC):
                                        te.matmul(pg[:, 0:_icw], qxT[:, dc, _jj * P:(_jj + 1) * P],
                                                  wgs[_kb][:, dc, 0:_icw],
                                                  start=(dc == 0), stop=(dc == DC - 1))
                                    for dc in range(DC):
                                        ins = te.matmul(py[:, 0:_icw], qxT[:, dc, _jj * P:(_jj + 1) * P],
                                                        wus[_kb][:, dc, 0:_icw],
                                                        start=(dc == 0), stop=(dc == DC - 1))
                                    return ins
                                step("tensor", f_mm, waits=(lg, lu))
                                def f_ev(v, _i=i, _icw=icw):
                                    v.tensor_scalar(gsc[:, 0:_icw], pg[:, 0:_icw], rsqall[:, _i:_i + 1], None, op0=OP.mult)
                                    vsync(v, v.tensor_scalar(ysc[:, 0:_icw], py[:, 0:_icw], rsqall[:, _i:_i + 1], None, op0=OP.mult))
                                    return v.tensor_copy(rdum2[:, 0:4], ysc[:, 0:4])
                                step("vector", f_ev)
                                step("scalar", lambda s, _icw=icw: s.activation(gsl[:, 0:_icw], gsc[:, 0:_icw], AF.Silu))
                                step("vector", lambda v, _icw=icw: v.scalar_tensor_tensor(
                                    hbuf[:, 0:_icw], gsl[:, 0:_icw], 36.0, ysc[:, 0:_icw],
                                    op0=OP.mult, op1=OP.mult))
                                if dbg and ci == 0 and ic == 0 and jj == 0:
                                    dh = dt("dbg_h", [P, 512], F32, kind="ExternalOutput").ap()
                                    s_ = dma("scalar", dh, hbuf[:], after=tick())
                                    step("sync", lambda q, _s=s_: q.nop(), waits=(s_,))
                                quant_tile(hbuf[:, 0:icw], hqb, icw, zh, pre6=True)
                                if dbg and ci == 0 and ic == 0 and jj == 0:
                                    dhq = dt("dbg_hq", [P, 512], BF16, kind="ExternalOutput").ap()
                                    s_ = dma("scalar", dhq, hqb[:], after=tick())
                                    step("sync", lambda q, _s=s_: q.nop(), waits=(s_,))
                                    for _dn, _dt_ in (("dbg_r6", zh["bm"]), ("dbg_s6h", zh["s6"]),
                                                      ("dbg_u", zh["u"]), ("dbg_y2", zh["y2"])):
                                        _o = dt(_dn, [P, _dt_.shape[1]], F32, kind="ExternalOutput").ap()
                                        _s2 = dma("scalar", _o, _dt_[:], after=tick())
                                        step("sync", (lambda _ss: (lambda q: q.nop()))(_s2), waits=(_s2,))
                                def f_th(te, _icw=icw):
                                    for sub in range(_icw // P):
                                        ins = te.transpose(ptb[:, sub * P:(sub + 1) * P],
                                                           hqb[:, sub * P:(sub + 1) * P], idb[:])
                                    return ins
                                step("tensor", f_th)
                                step("vector", lambda v, _ic=ic, _icw=icw, _jj=jj: v.tensor_copy(
                                    hqT[:, _ic * 4:_ic * 4 + _icw // P, _jj * P:(_jj + 1) * P],
                                    ptb[:, 0:_icw].rearrange("p (s t) -> p s t", t=P)))
                            wfree["g"][kb] = tick()
                            wfree["u"][kb] = tick()

                        for dch in range(DC):
                            kb = kdn % 2
                            ldw = dma("gpsimd", wdd[kb][:, :, :],
                                      qwdT.rearrange("(j p) d -> p j d", p=P)[:, :, dch * P:(dch + 1) * P],
                                      after=wfree["d"][kb])
                            kdn += 1
                            def f_dn(te, _kb=kb, _nt=ntile):
                                for j in range(NI0):
                                    ins = te.matmul(pd[:, 0:_nt * P], wdd[_kb][:, j, :],
                                                    hqT[:, j, 0:_nt * P],
                                                    start=(j == 0), stop=(j == NI0 - 1))
                                return ins
                            step("tensor", f_dn, waits=(ldw,))
                            if dbg and ci == 0 and dch == 0:
                                dpd = dt("dbg_pd", [P, 512], F32, kind="ExternalOutput").ap()
                                dpb = ph.enter_context(sbuf_t("dbg_pdb", [P, 512], F32))
                                step("vector", lambda v: v.tensor_copy(dpb[:], pd[:]))
                                s_ = dma("scalar", dpd, dpb[:], after=tick())
                                step("sync", lambda q, _s=s_: q.nop(), waits=(s_,))
                                dhqT = dt("dbg_hqT", [P, NI0, 512], BF16, kind="ExternalOutput").ap()
                                s2_ = dma("scalar", dhqT.rearrange("p j t -> p (j t)"), hqT[:].rearrange("p j t -> p (j t)"), after=tick())
                                step("sync", lambda q, _s=s2_: q.nop(), waits=(s2_,))
                            def f_acc(v, _dch=dch, _t0=t0, _nt=ntile):
                                if _t0 < NT:
                                    dst = stream[:, _dch, _t0 * P:(_t0 + _nt) * P]
                                else:
                                    dst = shalo[:, _dch, :]
                                return v.scalar_tensor_tensor(dst, pd[:, 0:_nt * P], 0.5, dst,
                                                              op0=OP.mult, op1=OP.add)
                            step("vector", f_acc)
                            wfree["d"][kb] = tick()

            def ph_l0attn(dbg=False):
                with ExitStack() as ph:
                    ph_lnstats(ph, NTH, "stream")

                with ExitStack() as ph:
                    xlt = ph.enter_context(sbuf_t("a_xlt", [P, NTH, D], BF16))
                    xlT = ph.enter_context(sbuf_t("a_xlT", [P, DC, TH], BF16))
                    xtk = ph.enter_context(sbuf_t("a_xtk", [P, D], F32))
                    wcb = ph.enter_context(sbuf_t("a_wcb", [P, DC, 32], BF16))
                    bcb = ph.enter_context(sbuf_t("a_bcb", [32, 1], F32))
                    lgt = ph.enter_context(sbuf_t("a_lgt", [P, 32], F32))
                    bcr_b = ph.enter_context(sbuf_t("a_bcr", [1, 32], BF16))
                    bcb2 = ph.enter_context(sbuf_t("a_bcb2", [P, 32], F32))
                    shm = ph.enter_context(sbuf_t("a_shm", [P, 160], F32))
                    shb = ph.enter_context(sbuf_t("a_shb", [P, 160], F32))
                    nmx = ph.enter_context(sbuf_t("a_nmx", [P, 1], F32))
                    ssum = ph.enter_context(sbuf_t("a_ssum", [P, 1], F32))
                    ssum2 = ph.enter_context(sbuf_t("a_ssum2", [P, 1], F32))
                    mt = ph.enter_context(sbuf_t("a_mt", [P, 160], BF16))
                    mla = ph.enter_context(sbuf_t("a_mla", [P, P], BF16))
                    mlb = ph.enter_context(sbuf_t("a_mlb", [32, P], BF16))
                    ptf = ph.enter_context(psum_t("a_ptf", [P, D], F32))
                    ptb = ph.enter_context(psum_t("a_ptb", [P, D], BF16))
                    psl2 = ph.enter_context(psum_t("a_psl2", [P, 32], F32))
                    ptm = ph.enter_context(psum_t("a_ptm", [P, P], BF16))
                    ptm2 = ph.enter_context(psum_t("a_ptm2", [32, P], BF16))
                    psm = ph.enter_context(psum_t("a_psm", [P, P], F32))

                    lw = dma("gpsimd", wcb[:], wcT.rearrange("(dc p) k -> p dc k", p=P))
                    lb = dma("sync", bcb[:], bconv)
                    lsh = dma("sync", shm[:], shmd)

                    for i in range(NTH):
                        def f_t(te, _i=i):
                            for dc in range(DC):
                                src = stream[:, dc, _i * P:(_i + 1) * P] if _i < NT else shalo[:, dc, :]
                                ins = te.transpose(ptf[:, dc * P:(dc + 1) * P], src, idf[:])
                            return ins
                        step("tensor", f_t)
                        step("vector", lambda v: v.tensor_copy(xtk[:], ptf[:]))
                        def f_n(v, _i=i):
                            vsync(v, v.tensor_scalar(
                                xlt[:, _i, :], xtk[:], mall[:, _i:_i + 1], rsqall[:, _i:_i + 1],
                                op0=OP.subtract, op1=OP.mult))
                            return v.tensor_copy(rdum2[:, 0:4], xlt[:, _i, 0:4])
                        step("vector", f_n)
                        def f_t2(te, _i=i):
                            for dc in range(DC):
                                ins = te.transpose(ptb[:, dc * P:(dc + 1) * P],
                                                   xlt[:, _i, dc * P:(dc + 1) * P], idb[:])
                            return ins
                        step("tensor", f_t2)
                        def f_c(v, _i=i):
                            vsync(v, v.tensor_copy(
                                xlT[:, :, _i * P:(_i + 1) * P],
                                ptb[:].rearrange("p (dc t) -> p dc t", t=P)))
                            return v.tensor_copy(rdum2[:, 0:4], xlT[:, 0, _i * P:_i * P + 4])
                        step("vector", f_c)

                    lbr = dma("gpsimd", bcr_b[:], bcr)
                    step("tensor", lambda te: te.matmul(psl2[:], onesb[:], bcr_b[:],
                                                        start=True, stop=True), waits=(lbr,))
                    step("vector", lambda v: v.tensor_copy(bcb2[:], psl2[:]))
                    if dbg:
                        dmv = dt("dbg_mr", [P, NTH, 2], F32, kind="ExternalOutput").ap()
                        s4_ = dma("scalar", dmv.rearrange("p a b -> p (a b)"),
                                  None, after=tick()) if False else None
                        dml = dt("dbg_mall", [P, NTH], F32, kind="ExternalOutput").ap()
                        s5_ = dma("scalar", dml, mall[:, 0:NTH], after=tick())
                        step("sync", (lambda _ss: (lambda q: q.nop()))(s5_), waits=(s5_,))
                        drs = dt("dbg_rsq", [P, NTH], F32, kind="ExternalOutput").ap()
                        s6_ = dma("scalar", drs, rsqall[:, 0:NTH], after=tick())
                        step("sync", (lambda _ss: (lambda q: q.nop()))(s6_), waits=(s6_,))
                        dxt = dt("dbg_xlt", [P, 2, D], BF16, kind="ExternalOutput").ap()
                        s7_ = dma("scalar", dxt, xlt[:, 0:2, :], after=tick())
                        step("sync", (lambda _ss: (lambda q: q.nop()))(s7_), waits=(s7_,))
                        dxT = dt("dbg_xlT", [P, DC, 256], BF16, kind="ExternalOutput").ap()
                        s8_ = dma("scalar", dxT, xlT[:, :, 0:256], after=tick())
                        step("sync", (lambda _ss: (lambda q: q.nop()))(s8_), waits=(s8_,))
                    last_ls = None
                    for i in range(NTH):
                        def f_lg(te, _i=i):
                            for dc in range(DC):
                                ins = te.matmul(psl2[:], xlT[:, dc, _i * P:(_i + 1) * P],
                                                wcb[:, dc, :], start=(dc == 0), stop=(dc == DC - 1))
                            return ins
                        step("tensor", f_lg, waits=(lw,))
                        step("vector", lambda v: v.tensor_tensor(lgt[:], psl2[:], bcb2[:], op=OP.add),
                             waits=(last_ls,) if last_ls else ())
                        if dbg and i == 0:
                            dl0 = dt("dbg_lg0", [P, 32], F32, kind="ExternalOutput").ap()
                            s9_ = dma("scalar", dl0, lgt[:], after=tick())
                            step("sync", (lambda _ss: (lambda q: q.nop()))(s9_), waits=(s9_,))
                        last_ls = dma("sync", logd[i * P:(i + 1) * P, :], lgt[:], after=tick())

                    for i in range(NT):
                        flat = logd.rearrange("a b -> (a b)")
                        sl = flat[i * P * 32:]
                        sl.ap[0] = [31, 128]
                        sl.ap.append([1, 160])
                        lsh_i = dma("sync", shb[:], sl, after=tick(), waits=(last_ls,))
                        def f_sm(v):
                            v.tensor_tensor(shb[:], shb[:], shm[:], op=OP.add)
                            vsync(v, v.tensor_reduce(nmx[:], shb[:], axis=AX.X, op=OP.max, negate=True))
                            return v.tensor_copy(rdum2[:, 0:1], nmx[:])
                        step("vector", f_sm, waits=(lsh_i, lsh))
                        def f_ex2(s):
                            ssync(s, s.activation(shb[:], shb[:], AF.Exp,
                                                  bias=nmx[:], accum_out=ssum[:]))
                            return s.activation(rdum2[:, 0:4], rdum[0:P, 0:4], AF.Copy)
                        step("scalar", f_ex2)
                        def f_nm(v):
                            vrecip(v, ssum2[:], ssum[:])
                            return v.tensor_scalar(mt[:], shb[:], ssum2[:], None, op0=OP.mult)
                        step("vector", f_nm)
                        def f_tm(te):
                            te.transpose(ptm[:], mt[:, 0:P], idb[:])
                            return te.transpose(ptm2[:], mt[:, P:160], idb[:])
                        step("tensor", f_tm)
                        def f_em(v):
                            v.tensor_copy(mla[:], ptm[:])
                            return v.tensor_copy(mlb[:], ptm2[:])
                        step("vector", f_em)
                        for dc in range(DC):
                            def f_mx(te, _i=i, _dc=dc):
                                te.matmul(psm[:], xlt[:, _i, _dc * P:(_dc + 1) * P], mla[:],
                                          start=True, stop=False)
                                return te.matmul(psm[:], xlt[0:32, _i + 1, _dc * P:(_dc + 1) * P],
                                                 mlb[:], start=False, stop=True)
                            step("tensor", f_mx)
                            step("vector", lambda v, _i=i, _dc=dc: v.tensor_tensor(
                                stream[:, _dc, _i * P:(_i + 1) * P],
                                stream[:, _dc, _i * P:(_i + 1) * P], psm[:], op=OP.add))

            def ph_moe(call, dbg=False):
                with ExitStack() as ph:
                    ph_lnstats(ph, NT, "stream")

                with ExitStack() as phx:
                    xlT = phx.enter_context(sbuf_t("m_xlT", [P, DC, T], BF16))
                    b2b = phx.enter_context(sbuf_t("m_b2b", [NE, D], BF16))
                    b1s = phx.enter_context(sbuf_t("m_b1s", [P, 16, NE], F32))
                    acc = phx.enter_context(sbuf_t("m_acc", [P, DC, 1024], F32))
                    cwb = phx.enter_context(sbuf_t("m_cwb", [P, 1024], F32))
                    asil = phx.enter_context(sbuf_t("m_asil", [P, 8, 1024], BF16))
                    btmp = phx.enter_context(sbuf_t("m_btmp", [P, 512], BF16))
                    w1s = [phx.enter_context(sbuf_t(f"m_w1{i}", [P, DC, P], BF16)) for i in range(2)]
                    w2b = phx.enter_context(sbuf_t("m_w2b", [P, DC, D], BF16))
                    with ExitStack() as ph:
                        xtk = ph.enter_context(sbuf_t("m_xtk", [P, D], F32))
                        xlt1 = ph.enter_context(sbuf_t("m_xlt", [P, D], BF16))
                        ptf = ph.enter_context(psum_t("m_ptf", [P, D], F32))
                        ptb = ph.enter_context(psum_t("m_ptb", [P, D], BF16))
                        for i in range(NT):
                            def f_t(te, _i=i):
                                for dc in range(DC):
                                    ins = te.transpose(ptf[:, dc * P:(dc + 1) * P],
                                                       stream[:, dc, _i * P:(_i + 1) * P], idf[:])
                                return ins
                            step("tensor", f_t)
                            step("vector", lambda v: v.tensor_copy(xtk[:], ptf[:]))
                            step("vector", lambda v, _i=i: v.tensor_scalar(
                                xlt1[:], xtk[:], mall[:, _i:_i + 1], rsqall[:, _i:_i + 1],
                                op0=OP.subtract, op1=OP.mult))
                            def f_t2(te):
                                for dc in range(DC):
                                    ins = te.transpose(ptb[:, dc * P:(dc + 1) * P],
                                                       xlt1[:, dc * P:(dc + 1) * P], idb[:])
                                return ins
                            step("tensor", f_t2)
                            step("vector", lambda v, _i=i: v.tensor_copy(
                                xlT[:, :, _i * P:(_i + 1) * P],
                                ptb[:].rearrange("p (dc t) -> p dc t", t=P)))

                    cw9 = phx.enter_context(sbuf_t("m_cw9", [NE, T], BF16))
                    ones2k = phx.enter_context(sbuf_t("m_on2k", [1, T], BF16))
                    cwr = phx.enter_context(sbuf_t("m_cwr", [1, 1024], BF16))
                    with ExitStack() as ph:
                        gwb = ph.enter_context(sbuf_t("m_gwb", [P, DC, 8], BF16))
                        gbb = ph.enter_context(sbuf_t("m_gbb", [P, 8], F32))
                        gbf = ph.enter_context(sbuf_t("m_gbf", [1, 8], BF16))
                        lt = ph.enter_context(sbuf_t("m_lt", [P, 8], F32))
                        pr = ph.enter_context(sbuf_t("m_pr", [P, 8], F32))
                        t8 = ph.enter_context(sbuf_t("m_t8", [P, 8], F32))
                        sc1 = ph.enter_context(sbuf_t("m_sc1", [P, 1], F32))
                        sc2 = ph.enter_context(sbuf_t("m_sc2", [P, 1], F32))
                        sc3 = ph.enter_context(sbuf_t("m_sc3", [P, 1], F32))
                        m1b = ph.enter_context(sbuf_t("m_m1b", [P, 1], F32))
                        m2b = ph.enter_context(sbuf_t("m_m2b", [P, 1], F32))
                        t8b = ph.enter_context(sbuf_t("m_t8b", [P, 8], F32))
                        cwt = ph.enter_context(sbuf_t("m_cwt", [P, 8], BF16))
                        psg = ph.enter_context(psum_t("m_psg", [P, 8], F32))
                        psb = ph.enter_context(psum_t("m_psb", [P, 8], F32))
                        pst8 = ph.enter_context(psum_t("m_pst8", [8, P], BF16))

                        lgw = dma("gpsimd", gwb[:], gwd.rearrange("(dc p) e -> p dc e", p=P))
                        lgb = dma("gpsimd", gbf[:], gbd)
                        step("tensor", lambda te: te.matmul(psb[:], onesb[:], gbf[:],
                                                            start=True, stop=True), waits=(lgb,))
                        step("vector", lambda v: v.tensor_copy(gbb[:], psb[:]))
                        step("gpsimd", lambda g: g.memset(ones2k[:], 1.0))
                        l1s = dma("gpsimd", cw9[8:9, :], ones2k[:], after=tick())
                        step("sync", (lambda _ss: (lambda q: q.nop()))(l1s), waits=(l1s,))
                        for i in range(NT):
                            def f_g(te, _i=i):
                                for dc in range(DC):
                                    ins = te.matmul(psg[:], xlT[:, dc, _i * P:(_i + 1) * P],
                                                    gwb[:, dc, :], start=(dc == 0), stop=(dc == DC - 1))
                                return ins
                            step("tensor", f_g, waits=(lgw,))
                            def f_sm(v):
                                v.tensor_tensor(lt[:], psg[:], gbb[:], op=OP.add)
                                vsync(v, v.tensor_reduce(sc1[:], lt[:], axis=AX.X, op=OP.max, negate=True))
                                return v.tensor_copy(rdum2[:, 0:1], sc1[:])
                            step("vector", f_sm)
                            def f_ex(s):
                                ssync(s, s.activation(lt[:], lt[:], AF.Exp,
                                                      bias=sc1[:], accum_out=sc2[:]))
                                return s.activation(rdum2[:, 0:4], rdum[0:P, 0:4], AF.Copy)
                            step("scalar", f_ex)
                            def f_tp(v):
                                vrecip(v, sc3[:], sc2[:])
                                vsync(v, v.tensor_scalar(pr[:], lt[:], sc3[:], None, op0=OP.mult))
                                vsync(v, v.tensor_reduce(m1b[:], pr[:], axis=AX.X, op=OP.max))
                                vsync(v, v.tensor_scalar(t8[:], pr[:], m1b[:], None, op0=OP.is_lt))
                                vsync(v, v.tensor_tensor(t8b[:], pr[:], t8[:], op=OP.mult))
                                vsync(v, v.tensor_reduce(m2b[:], t8b[:], axis=AX.X, op=OP.max))
                                vsync(v, v.tensor_scalar(t8[:], pr[:], m2b[:], None, op0=OP.is_ge))
                                vsync(v, v.tensor_tensor(cwt[:], pr[:], t8[:], op=OP.mult))
                                return v.tensor_copy(rdum2[:, 0:8], cwt[:])
                            step("vector", f_tp)
                            if dbg and i == 0:
                                for _dn, _dt_ in (("dbg_lt", lt), ("dbg_pr", pr), ("dbg_cwt", cwt),
                                                  ("dbg_sc2", sc2), ("dbg_sc3", sc3)):
                                    _w = _dt_.shape[1]
                                    _o = dt(_dn, [P, _w], F32 if _dt_.dtype == F32 else BF16, kind="ExternalOutput").ap()
                                    _sD = dma("scalar", _o, _dt_[:], after=tick())
                                    step("sync", (lambda _ss: (lambda q: q.nop()))(_sD), waits=(_sD,))
                            step("tensor", lambda te: te.transpose(pst8[:], cwt[:], idb[:]))
                            step("vector", lambda v, _i=i: v.tensor_copy(
                                cw9[0:8, _i * P:(_i + 1) * P], pst8[0:8, :]))

                    with ExitStack() as ph:
                        psu = ph.enter_context(psum_t("m_psu", [P, 512], F32))
                        psd = ph.enter_context(psum_t("m_psd", [P, 512], F32))
                        psc = ph.enter_context(psum_t("m_psc", [P, 512], F32))
                        psi = ph.enter_context(psum_t("m_psi", [P, 512], F32))

                        if dbg:
                            dcw = dt("dbg_cw9", [NE, T], BF16, kind="ExternalOutput").ap()
                            sA = dma("scalar", dcw, cw9[:], after=tick())
                            step("sync", (lambda _ss: (lambda q: q.nop()))(sA), waits=(sA,))
                        lb2 = dma("gpsimd", b2b[:], b2d)
                        lb1 = dma("sync", b1s[:], b1Td.rearrange("(fc p) e -> p fc e", p=P))
                        w1free = [0, 0]
                        w2free = 0
                        kw = 0
                        for r in range(2):
                            tok0 = r * 1024
                            for dc2 in range(DC):
                                for th in range(2):
                                    def f_i(te, _dc2=dc2, _th=th, _t0=tok0):
                                        return te.matmul(psi[:], b2b[:, _dc2 * P:(_dc2 + 1) * P],
                                                         cw9[:, _t0 + _th * 512:_t0 + (_th + 1) * 512],
                                                         start=True, stop=True)
                                    step("tensor", f_i, waits=(lb2,))
                                    step("vector", lambda v, _dc2=dc2, _th=th: v.tensor_copy(
                                        acc[:, _dc2, _th * 512:(_th + 1) * 512], psi[:]))
                            if dbg and r == 0:
                                db2 = dt("dbg_b2b", [NE, D], BF16, kind="ExternalOutput").ap()
                                sE = dma("scalar", db2, b2b[:], after=tick())
                                step("sync", (lambda _ss: (lambda q: q.nop()))(sE), waits=(sE,))
                                dai = dt("dbg_acci", [P, DC, 1024], F32, kind="ExternalOutput").ap()
                                sB = dma("scalar", dai.rearrange("p a b -> p (a b)"),
                                         acc[:].rearrange("p a b -> p (a b)"), after=tick())
                                step("sync", (lambda _ss: (lambda q: q.nop()))(sB), waits=(sB,))
                            for e in range(NE):
                                lw2 = dma("gpsimd", w2b[:],
                                          w2d[e].rearrange("(fj p) d2 -> p fj d2", p=P),
                                          after=w2free)
                                if e > 0:
                                    lcw = dma("sync", cwr[0:1, :],
                                              cw9[e - 1:e, tok0:tok0 + 1024], after=tick())
                                    for th in range(2):
                                        def f_cb(te, _th=th):
                                            return te.matmul(psc[:], onesb[:],
                                                             cwr[0:1, _th * 512:(_th + 1) * 512],
                                                             start=True, stop=True)
                                        step("tensor", f_cb, waits=(lcw,))
                                        step("vector", lambda v, _th=th: v.tensor_copy(
                                            cwb[:, _th * 512:(_th + 1) * 512], psc[:]))
                                for fj in range(16):
                                    kb = kw % 2
                                    lw1 = dma("gpsimd", w1s[kb][:],
                                              w1d[e].rearrange("(dc p) f -> p dc f", p=P)[:, :, fj * P:(fj + 1) * P],
                                              after=w1free[kb])
                                    kw += 1
                                    for th in range(2):
                                        def f_u(te, _kb=kb, _th=th, _t0=tok0):
                                            for dc in range(DC):
                                                ins = te.matmul(psu[:], w1s[_kb][:, dc, :],
                                                                xlT[:, dc, _t0 + _th * 512:_t0 + (_th + 1) * 512],
                                                                start=(dc == 0), stop=(dc == DC - 1))
                                            return ins
                                        step("tensor", f_u, waits=(lw1, lb1))
                                        if fj < 8:
                                            def f_si(s, _fj=fj, _th=th, _e=e):
                                                ssync(s, s.activation(
                                                    asil[:, _fj, _th * 512:(_th + 1) * 512], psu[:],
                                                    AF.Silu, bias=b1s[:, _fj, _e:_e + 1]))
                                                return s.activation(rdum2[:, 0:4], rdum[0:P, 0:4], AF.Copy)
                                            step("scalar", f_si)
                                        else:
                                            if e > 0:
                                                def f_b(v, _fj=fj, _th=th, _e=e):
                                                    vsync(v, v.tensor_scalar(
                                                        btmp[:], psu[:], b1s[:, _fj, _e:_e + 1], None, op0=OP.add))
                                                    vsync(v, v.tensor_tensor(
                                                        btmp[:], btmp[:], cwb[:, _th * 512:(_th + 1) * 512], op=OP.mult))
                                                    return v.tensor_copy(rdum2[:, 0:4], btmp[:, 0:4])
                                            else:
                                                def f_b(v, _fj=fj, _th=th, _e=e):
                                                    vsync(v, v.tensor_scalar(
                                                        btmp[:], psu[:], b1s[:, _fj, _e:_e + 1], None, op0=OP.add))
                                                    return v.tensor_copy(rdum2[:, 0:4], btmp[:, 0:4])
                                            step("vector", f_b)
                                            step("vector", lambda v, _fj=fj, _th=th: v.tensor_tensor(
                                                asil[:, _fj - 8, _th * 512:(_th + 1) * 512],
                                                asil[:, _fj - 8, _th * 512:(_th + 1) * 512],
                                                btmp[:], op=OP.mult))
                                    w1free[kb] = tick()
                                if dbg and r == 0 and e == 0:
                                    das = dt("dbg_asil", [P, 8, 1024], BF16, kind="ExternalOutput").ap()
                                    sC = dma("scalar", das.rearrange("p a b -> p (a b)"),
                                             asil[:].rearrange("p a b -> p (a b)"), after=tick())
                                    step("sync", (lambda _ss: (lambda q: q.nop()))(sC), waits=(sC,))
                                for dc2 in range(DC):
                                    for th in range(2):
                                        def f_d(te, _dc2=dc2, _th=th):
                                            for fj2 in range(DC):
                                                ins = te.matmul(psd[:], w2b[:, fj2, _dc2 * P:(_dc2 + 1) * P],
                                                                asil[:, fj2, _th * 512:(_th + 1) * 512],
                                                                start=(fj2 == 0), stop=(fj2 == DC - 1))
                                            return ins
                                        step("tensor", f_d, waits=(lw2,))
                                        step("vector", lambda v, _dc2=dc2, _th=th: v.tensor_tensor(
                                            acc[:, _dc2, _th * 512:(_th + 1) * 512],
                                            acc[:, _dc2, _th * 512:(_th + 1) * 512],
                                            psd[:], op=OP.add))
                                w2free = tick()
                            for dc2 in range(DC):
                                def f_fin(v, _dc2=dc2, _t0=tok0):
                                    dst = stream[:, _dc2, _t0:_t0 + 1024]
                                    return v.scalar_tensor_tensor(dst, acc[:, _dc2, :], 0.5, dst,
                                                                  op0=OP.mult, op1=OP.add)
                                step("vector", f_fin)

            def ph_attn(dbg=False):
                with ExitStack() as ph:
                    ph_lnstats(ph, NT, "stream")

                with ExitStack() as ph:
                    xlT = ph.enter_context(sbuf_t("t_xlT", [P, DC, T], BF16))
                    xtk = ph.enter_context(sbuf_t("t_xtk", [P, D], F32))
                    xlt1 = ph.enter_context(sbuf_t("t_xlt", [P, D], BF16))
                    ptf = ph.enter_context(psum_t("t_ptf", [P, D], F32))
                    ptb = ph.enter_context(psum_t("t_ptb", [P, D], BF16))
                    for i in range(NT):
                        def f_t(te, _i=i):
                            for dc in range(DC):
                                ins = te.transpose(ptf[:, dc * P:(dc + 1) * P],
                                                   stream[:, dc, _i * P:(_i + 1) * P], idf[:])
                            return ins
                        step("tensor", f_t)
                        step("vector", lambda v: v.tensor_copy(xtk[:], ptf[:]))
                        step("vector", lambda v, _i=i: v.tensor_scalar(
                            xlt1[:], xtk[:], mall[:, _i:_i + 1], rsqall[:, _i:_i + 1],
                            op0=OP.subtract, op1=OP.mult))
                        def f_t2(te):
                            for dc in range(DC):
                                ins = te.transpose(ptb[:, dc * P:(dc + 1) * P],
                                                   xlt1[:, dc * P:(dc + 1) * P], idb[:])
                            return ins
                        step("tensor", f_t2)
                        step("vector", lambda v, _i=i: v.tensor_copy(
                            xlT[:, :, _i * P:(_i + 1) * P],
                            ptb[:].rearrange("p (dc t) -> p dc t", t=P)))
                    sx = dma("scalar", ccxi.rearrange("(dc p) t -> p dc t", p=P), xlT[:],
                             after=tick())
                    step("sync", lambda q: q.nop(), waits=(sx,))

                n0 = tick()
                def f_cc(g):
                    g.wait_ge(tk, n0)
                    g.collective_compute(
                        "AllGather", OP.bypass, replica_groups=RG,
                        ins=[ccxi.opt()], outs=[ccxo.opt()],
                    ).then_inc(ccs, 1)
                blk.gpsimd(f_cc)
                st["cc"] += 1
                ccw = (ccs, st["cc"])

                settle()
                with ExitStack() as ph:
                    wqb = ph.enter_context(sbuf_t("t_wqb", [P, DC, 512], BF16))
                    wkb = ph.enter_context(sbuf_t("t_wkb", [P, DC, 512], BF16))
                    wvb = ph.enter_context(sbuf_t("t_wvb", [P, DC, 512], BF16))
                    bqs = ph.enter_context(sbuf_t("t_bqs", [P, 4], F32))
                    bks = ph.enter_context(sbuf_t("t_bks", [P, 4], F32))
                    bvs = ph.enter_context(sbuf_t("t_bvs", [P, 4], F32))
                    ams = ph.enter_context(sbuf_t("t_ams", [P, 4, 512], BF16))
                    kT = ph.enter_context(sbuf_t("t_kT", [P, 4, 4096], BF16))
                    vs = ph.enter_context(sbuf_t("t_vs", [P, 32, 520], BF16))
                    qT = ph.enter_context(sbuf_t("t_qT", [P, 4, 512], BF16))
                    xch = ph.enter_context(sbuf_t("t_xch", [P, DC, 512], BF16))
                    pex1 = ph.enter_context(sbuf_t("t_pex1", [P, 512], BF16))
                    pex2 = ph.enter_context(sbuf_t("t_pex2", [P, 512], BF16))
                    ypr = ph.enter_context(sbuf_t("t_ypr", [P, 512], BF16))
                    rc1 = ph.enter_context(sbuf_t("t_rc1", [1, 512], F32))
                    rc2 = ph.enter_context(sbuf_t("t_rc2", [1, 512], F32))
                    rc1b = ph.enter_context(sbuf_t("t_rc1b", [1, 512], F32))
                    rc2b = ph.enter_context(sbuf_t("t_rc2b", [1, 512], F32))
                    rc1c = ph.enter_context(sbuf_t("t_rc1c", [1, 512], BF16))
                    rc2c = ph.enter_context(sbuf_t("t_rc2c", [1, 512], BF16))
                    rbs = ph.enter_context(sbuf_t("t_rbs", [64, 512], F32))
                    psq = ph.enter_context(psum_t("t_psq", [P, 512], F32))
                    psS1 = ph.enter_context(psum_t("t_psS1", [P, 512], F32))
                    psS2 = ph.enter_context(psum_t("t_psS2", [P, 512], F32))
                    psY1 = ph.enter_context(psum_t("t_psY1", [65, 512], F32))
                    psY2 = ph.enter_context(psum_t("t_psY2", [65, 512], F32))
                    psrb = ph.enter_context(psum_t("t_psrb", [64, 512], F32))

                    lq = dma("gpsimd", wqb[:], wqT.rearrange("(dc p) c -> p dc c", p=P))
                    lk = dma("gpsimd", wkb[:], wkT.rearrange("(dc p) c -> p dc c", p=P))
                    lv = dma("gpsimd", wvb[:], wvT.rearrange("(dc p) c -> p dc c", p=P))
                    lbq = dma("sync", bqs[:], bqd)
                    lbk = dma("sync", bks[:], bkd)
                    lbv = dma("sync", bvs[:], bvd)
                    lam = dma("sync", ams[:], amd.rearrange("j p q -> p j q"))
                    def f_ones(g):
                        for h in range(8):
                            ins = g.memset(vs[:, :, h * 65 + 64:h * 65 + 65], 1.0)
                        return ins
                    step("gpsimd", f_ones)
                    if dbg:
                        dbv = dt("dbg_bvb", [P, 4], F32, kind="ExternalOutput").ap()
                        sBV = dma("scalar", dbv, bvs[:], after=tick())
                        step("sync", (lambda _ss: (lambda q: q.nop()))(sBV), waits=(sBV,))

                    ylast = None
                    for tch in range(8):
                        half = tch // 4
                        c0 = (tch * 512) % T
                        lx = dma("sync", xch[:],
                                 ccxo[half, :, c0:c0 + 512].rearrange("(dc p) t -> p dc t", p=P),
                                 after=tick(), waits=(ccw,))
                        for pair in range(4):
                            def f_k(te, _p=pair):
                                for dc in range(DC):
                                    ins = te.matmul(psq[:], wkb[:, dc, _p * P:(_p + 1) * P],
                                                    xch[:, dc, :], start=(dc == 0), stop=(dc == DC - 1))
                                return ins
                            step("tensor", f_k, waits=(lx, lk, lbk))
                            def f_ke(v, _p=pair, _t=tch):
                                vsync(v, v.tensor_scalar(
                                    kT[:, _p, _t * 512:(_t + 1) * 512], psq[:], bks[:, _p:_p + 1], None,
                                    op0=OP.add))
                                return v.tensor_copy(rdum2[:, 0:4], kT[:, _p, 0:4])
                            step("vector", f_ke)
                            def f_q(te, _p=pair):
                                for dc in range(DC):
                                    ins = te.matmul(psq[:], wqb[:, dc, _p * P:(_p + 1) * P],
                                                    xch[:, dc, :], start=(dc == 0), stop=(dc == DC - 1))
                                return ins
                            step("tensor", f_q, waits=(lq, lbq))
                            def f_qe(v, _p=pair):
                                vsync(v, v.tensor_scalar(
                                    qT[:, _p, :], psq[:], bqs[:, _p:_p + 1], None,
                                    op0=OP.add))
                                return v.tensor_copy(rdum2[:, 0:4], qT[:, _p, 0:4])
                            step("vector", f_qe)
                        for tt in range(4):
                            def f_v(te, _tt=tt):
                                for dc in range(DC):
                                    ins = te.matmul(psq[:], xch[:, dc, _tt * P:(_tt + 1) * P],
                                                    wvb[:, dc, :], start=(dc == 0), stop=(dc == DC - 1))
                                return ins
                            step("tensor", f_v, waits=(lv,))
                            def f_vev(v, _tt=tt, _t=tch):
                                for h in range(8):
                                    ins = v.tensor_copy(vs[:, _t * 4 + _tt, h * 65:h * 65 + 64],
                                                        psq[:, h * 64:(h + 1) * 64])
                                vsync(v, ins)
                                return v.tensor_copy(rdum2[:, 0:4], vs[:, _t * 4 + _tt, 0:4])
                            step("vector", f_vev)
                            if dbg and tch == 0 and tt == 0:
                                dps = dt("dbg_psqv", [P, 512], F32, kind="ExternalOutput").ap()
                                pvb = ph.enter_context(sbuf_t("t_pvb", [P, 512], F32))
                                def f_pv(v):
                                    vsync(v, v.tensor_copy(pvb[:], psq[:]))
                                    return v.tensor_copy(rdum2[:, 0:4], pvb[:, 0:4])
                                step("vector", f_pv)
                                sPV = dma("scalar", dps, pvb[:], after=tick())
                                step("sync", (lambda _ss: (lambda q: q.nop()))(sPV), waits=(sPV,))
                                dxc = dt("dbg_xch", [P, DC, 512], BF16, kind="ExternalOutput").ap()
                                sXC = dma("scalar", dxc.rearrange("p a b -> p (a b)"),
                                          xch[:].rearrange("p a b -> p (a b)"), after=tick())
                                step("sync", (lambda _ss: (lambda q: q.nop()))(sXC), waits=(sXC,))
                        nkb = 4 * tch + 4
                        for pair in range(4):
                            for kb in range(nkb):
                                def f_S(te, _p=pair, _kb=kb):
                                    te.matmul(psS1[:], kT[0:64, _p, _kb * P:(_kb + 1) * P],
                                              qT[0:64, _p, :], start=True, stop=True)
                                    return te.matmul(psS2[:], kT[64:128, _p, _kb * P:(_kb + 1) * P],
                                                     qT[64:128, _p, :], start=True, stop=True)
                                step("tensor", f_S)
                                if dbg and tch == 7 and pair == 0 and kb in (27, 31):
                                    dS = dt(f"dbg_psS_{kb}", [P, 512], F32, kind="ExternalOutput").ap()
                                    sbb = ph.enter_context(sbuf_t(f"t_dS{kb}", [P, 512], F32))
                                    def f_ds(v, _s=sbb):
                                        vsync(v, v.tensor_copy(_s[:], psS1[:]))
                                        return v.tensor_copy(rdum2[:, 0:4], _s[:, 0:4])
                                    step("vector", f_ds)
                                    sD = dma("scalar", dS, sbb[:], after=tick())
                                    step("sync", (lambda _ss: (lambda q: q.nop()))(sD), waits=(sD,))
                                def f_e(s):
                                    ssync(s, s.activation(pex1[:], psS1[:], AF.Exp,
                                                          bias=neg4[:], scale=0.125))
                                    ssync(s, s.activation(pex2[:], psS2[:], AF.Exp,
                                                          bias=neg4[:], scale=0.125))
                                    return s.activation(rdum2[:, 0:4], rdum[0:P, 0:4], AF.Copy)
                                step("scalar", f_e)
                                if kb >= 4 * tch:
                                    def f_m(v, _j=kb - 4 * tch):
                                        v.tensor_tensor(pex1[:], pex1[:], ams[:, _j, :], op=OP.mult)
                                        vsync(v, v.tensor_tensor(pex2[:], pex2[:], ams[:, _j, :], op=OP.mult))
                                        return v.tensor_copy(rdum2[:, 0:4], pex2[:, 0:4])
                                    step("vector", f_m, waits=(lam,))
                                if dbg and tch == 7 and pair == 0 and kb in (27, 31):
                                    dE = dt(f"dbg_pex_{kb}", [P, 512], BF16, kind="ExternalOutput").ap()
                                    seb = ph.enter_context(sbuf_t(f"t_dE{kb}", [P, 512], BF16))
                                    def f_de(v, _s=seb):
                                        vsync(v, v.tensor_copy(_s[:], pex1[:]))
                                        return v.tensor_copy(rdum2[:, 0:4], _s[:, 0:4])
                                    step("vector", f_de)
                                    sE = dma("scalar", dE, seb[:], after=tick())
                                    step("sync", (lambda _ss: (lambda q: q.nop()))(sE), waits=(sE,))
                                def f_av(te, _p=pair, _kb=kb, _n=nkb):
                                    te.matmul(psY1[:], vs[:, _kb, (2 * _p) * 65:(2 * _p) * 65 + 65],
                                              pex1[:], start=(_kb == 0), stop=(_kb == _n - 1))
                                    return te.matmul(psY2[:], vs[:, _kb, (2 * _p + 1) * 65:(2 * _p + 1) * 65 + 65],
                                                     pex2[:], start=(_kb == 0), stop=(_kb == _n - 1))
                                step("tensor", f_av)
                            if dbg and tch == 7 and pair == 0:
                                dY = dt("dbg_psY1", [65, 512], F32, kind="ExternalOutput").ap()
                                syb = ph.enter_context(sbuf_t("t_dY", [65, 512], F32))
                                def f_dy(v, _s=syb):
                                    vsync(v, v.tensor_copy(_s[:], psY1[:]))
                                    return v.tensor_copy(rdum2[0:4, 0:4], _s[0:4, 0:4])
                                step("vector", f_dy)
                                sY = dma("scalar", dY, syb[:], after=tick())
                                step("sync", (lambda _ss: (lambda q: q.nop()))(sY), waits=(sY,))
                            def f_rc(v):
                                vsync(v, v.tensor_copy(rc1[:], psY1[64:65, :]))
                                vsync(v, v.tensor_copy(rc2[:], psY2[64:65, :]))
                                vrecip(v, rc1b[:], rc1[:])
                                vrecip(v, rc2b[:], rc2[:])
                                vsync(v, v.tensor_copy(rc1c[:], rc1b[:]))
                                vsync(v, v.tensor_copy(rc2c[:], rc2b[:]))
                                return v.tensor_copy(rdum2[0:1, 0:4], rc2c[0:1, 0:4])
                            step("vector", f_rc, waits=(ylast,) if ylast else ())
                            step("tensor", lambda te: te.matmul(psrb[:], onesb[0:1, 0:64], rc1c[:],
                                                                start=True, stop=True))
                            def f_rb1(v):
                                vsync(v, v.tensor_copy(rbs[:], psrb[:]))
                                return v.tensor_copy(rdum2[0:64, 0:4], rbs[0:64, 0:4])
                            step("vector", f_rb1)
                            def f_yp1(v, _p=pair):
                                v.tensor_tensor(ypr[0:64, :], psY1[0:64, :], rbs[:], op=OP.mult)
                                vsync(v, v.tensor_scalar(ypr[0:64, :], ypr[0:64, :],
                                                         bvs[0:64, _p:_p + 1], None, op0=OP.add))
                                return v.tensor_copy(rdum2[0:4, 0:4], ypr[0:4, 0:4])
                            step("vector", f_yp1)
                            step("tensor", lambda te: te.matmul(psrb[:], onesb[0:1, 0:64], rc2c[:],
                                                                start=True, stop=True))
                            step("vector", f_rb1)
                            def f_yp2(v, _p=pair):
                                v.tensor_tensor(ypr[64:128, :], psY2[0:64, :], rbs[:], op=OP.mult)
                                vsync(v, v.tensor_scalar(ypr[64:128, :], ypr[64:128, :],
                                                         bvs[64:128, _p:_p + 1], None, op0=OP.add))
                                return v.tensor_copy(rdum2[0:4, 0:4], ypr[64:68, 0:4])
                            step("vector", f_yp2)
                            ylast = dma("scalar", ymy[pair * P:(pair + 1) * P, tch * 512:(tch + 1) * 512],
                                        ypr[:], after=tick())
                    if dbg:
                        for _dn, _dt_ in (("dbg_kT", kT), ("dbg_vs", vs), ("dbg_qT", qT),
                                          ("dbg_wqb", wqb), ("dbg_wvb", wvb)):
                            sh = _dt_.shape
                            _o = dt(_dn, list(sh), BF16, kind="ExternalOutput").ap()
                            _sD = dma("scalar", _o.rearrange("p a b -> p (a b)"),
                                      _dt_[:].rearrange("p a b -> p (a b)"), after=tick())
                            step("sync", (lambda _ss: (lambda q: q.nop()))(_sD), waits=(_sD,))
                    step("sync", lambda q: q.nop(), waits=(ylast,))

                settle()
                with ExitStack() as ph:
                    wpb = ph.enter_context(sbuf_t("t_wpb", [P, 4, D], BF16))
                    bps = ph.enter_context(sbuf_t("t_bps", [P, DC], F32))
                    ysb = ph.enter_context(sbuf_t("t_ysb", [P, 4, 512], BF16))
                    prt = ph.enter_context(sbuf_t("t_prt", [P, 512], F32))
                    pso = ph.enter_context(psum_t("t_pso", [P, 512], F32))
                    lwp = dma("gpsimd", wpb[:], wpT.rearrange("(dc p) c -> p dc c", p=P))
                    lbp = dma("sync", bps[:], bpd)
                    plast = None
                    for tch in range(8):
                        ly = dma("sync", ysb[:],
                                 ymy.rearrange("(c4 p) t -> p c4 t", p=P)[:, :, tch * 512:(tch + 1) * 512],
                                 after=tick())
                        for dc2 in range(DC):
                            def f_o(te, _dc2=dc2):
                                for dc in range(4):
                                    ins = te.matmul(pso[:], wpb[:, dc, _dc2 * P:(_dc2 + 1) * P],
                                                    ysb[:, dc, :], start=(dc == 0), stop=(dc == 3))
                                return ins
                            step("tensor", f_o, waits=(ly, lwp))
                            def f_pe(v):
                                vsync(v, v.tensor_copy(prt[:], pso[:]))
                                return v.tensor_copy(rdum2[:, 0:4], prt[:, 0:4])
                            step("vector", f_pe, waits=(plast,) if plast else ())
                            plast = dma("scalar",
                                        cri[tch // 4, dc2 * P:(dc2 + 1) * P,
                                            (tch % 4) * 512:((tch % 4) + 1) * 512],
                                        prt[:], after=tick())
                    step("sync", lambda q: q.nop(), waits=(plast,))

                n1 = tick()
                def f_cc2(g):
                    g.wait_ge(tk, n1)
                    g.collective_compute(
                        "ReduceScatter", OP.add, replica_groups=RG,
                        ins=[cri.opt()], outs=[cro.opt()],
                    ).then_inc(ccs, 1)
                blk.gpsimd(f_cc2)
                st["cc"] += 1
                ccw2 = (ccs, st["cc"])

                if dbg:
                    dsp = dt("dbg_spre", [P, DC, T], F32, kind="ExternalOutput").ap()
                    sSP = dma("scalar", dsp.rearrange("p a b -> p (a b)"),
                              stream[:].rearrange("p a b -> p (a b)"), after=tick())
                    step("sync", (lambda _ss: (lambda q: q.nop()))(sSP), waits=(sSP,))
                with ExitStack() as ph:
                    rsb = ph.enter_context(sbuf_t("t_rsb", [P, DC, 512], F32))
                    rtmp = ph.enter_context(sbuf_t("t_rtmp", [P, 512], F32))
                    bps2 = ph.enter_context(sbuf_t("t_bps2", [P, DC], F32))
                    lbp2 = dma("sync", bps2[:], bpd, after=tick())
                    for tc2 in range(4):
                        lr = dma("sync", rsb[:],
                                 cro.rearrange("(dc p) t -> p dc t", p=P)[:, :, tc2 * 512:(tc2 + 1) * 512],
                                 after=tick(), waits=(ccw2,))
                        for dc2 in range(DC):
                            def f_rb(v, _dc2=dc2):
                                vsync(v, v.tensor_scalar(
                                    rtmp[:], rsb[:, _dc2, :], bps2[:, _dc2:_dc2 + 1], None, op0=OP.add))
                                return v.tensor_copy(rdum2[:, 0:4], rtmp[:, 0:4])
                            step("vector", f_rb, waits=(lr, lbp2))
                            def f_r(v, _dc2=dc2, _tc2=tc2):
                                dst = stream[:, _dc2, _tc2 * 512:(_tc2 + 1) * 512]
                                return v.tensor_tensor(dst, dst, rtmp[:], op=OP.add)
                            step("vector", f_r)

            def ph_store():
                settle()
                with ExitStack() as ph:
                    ev = [ph.enter_context(sbuf_t(f"o_ev{i}", [P, D], F32)) for i in range(2)]
                    pso = ph.enter_context(psum_t("o_ps", [P, D], F32))
                    evst2 = [None, None]
                    last = None
                    for i in range(NT):
                        def f_t(te, _i=i):
                            for dc in range(DC):
                                ins = te.transpose(pso[:, dc * P:(dc + 1) * P],
                                                   stream[:, dc, _i * P:(_i + 1) * P], idf[:])
                            return ins
                        step("tensor", f_t)
                        step("vector", lambda v, _b=i % 2: v.tensor_copy(ev[_b][:], pso[:]),
                             waits=(evst2[i % 2],) if evst2[i % 2] else ())
                        last = dma("sync", xout[i * P:(i + 1) * P, :], ev[i % 2][:], after=tick())
                        evst2[i % 2] = last
                    step("sync", lambda q: q.nop(), waits=(last,))

            def ph_debug(items):
                for nm, ap in items:
                    rows, cols = ap.shape[-2], ap.shape[-1]
                    o = dt("dbg_" + nm, list(ap.shape), ap.dtype, kind="ExternalOutput").ap()
                    bl = sbt("dbg_" + nm + "_b", [P, cols], ap.dtype)
                    flat_in = ap.rearrange("... -> (...)").rearrange("(r c) -> r c", c=cols) if False else ap
                    n = 1
                    sh = list(ap.shape)
                    if len(sh) == 3:
                        n = sh[0]
                    for blk_i in range(n):
                        src_ = ap[blk_i] if len(sh) == 3 else ap
                        dst_ = o[blk_i] if len(sh) == 3 else o
                        for r0 in range(0, sh[-2], P):
                            r1 = min(sh[-2], r0 + P)
                            l = dma("sync", bl[0:r1 - r0, :], src_[r0:r1, :], after=tick())
                            s_ = dma("scalar", dst_[r0:r1, :], bl[0:r1 - r0, :], after=tick(), waits=(l,))
                            step("sync", lambda q, _s=s_: q.nop(), waits=(s_,))

            ph_consts()
            if stage >= 1:
                with ExitStack() as phw:
                    ph_wprep(phw)
                ph_l0ffn(1, dbg=bool(debug) and "s1" in debug)
            else:
                with ExitStack() as ph:
                    ph_lnstats(ph, NT, "dram", init_stream=True)
            if stage >= 2:
                ph_l0attn(dbg=bool(debug) and "s2" in debug)
            if stage >= 3:
                ph_l0ffn(2)
            if stage >= 4:
                ph_moe(4, dbg=bool(debug) and "s4" in debug)
            if stage >= 5:
                ph_attn(dbg=bool(debug) and "s5" in debug)
            if stage >= 6:
                ph_moe(6)
            if debug:
                dmap = {"qwgT": qwgT, "qwuT": qwuT, "qwdT": qwdT, "logd": logd,
                        "ccxi": ccxi, "ccxo": ccxo, "ymy": ymy, "cri": cri, "cro": cro}
                ph_debug([(nm, dmap[nm]) for nm in debug if nm in dmap])
            ph_store()

    return nc


def _stage(inputs):
    import ml_dtypes
    x = np.asarray(inputs["L0_attn_x"])
    idf = np.eye(P, dtype=np.float32)
    idb = np.eye(P, dtype=np.float32).astype(ml_dtypes.bfloat16)
    pidx = np.arange(P)[:, None]
    cidx = np.arange(160)[None, :]
    shm = np.where((cidx >= pidx) & (cidx < pidx + 32), 0.0, -1e30).astype(np.float32)
    am = np.zeros((4, P, 512), dtype=np.float32)
    for j in range(4):
        kk = np.arange(P)[:, None] + 128 * j
        qq = np.arange(512)[None, :]
        keep = (kk <= qq) & ~((qq % 2 == 1) & (kk == qq - 1))
        am[j] = keep.astype(np.float32)
    am = am.astype(ml_dtypes.bfloat16)

    w1 = np.ascontiguousarray(np.concatenate(
        [np.asarray(inputs["L1_ffn_sw1"]), np.asarray(inputs["L1_ffn_rw1"])], 0))
    w2 = np.ascontiguousarray(np.concatenate(
        [np.asarray(inputs["L1_ffn_sw2"]), np.asarray(inputs["L1_ffn_rw2"])], 0))
    b1T = np.ascontiguousarray(np.concatenate(
        [np.asarray(inputs["L1_ffn_sb1"]), np.asarray(inputs["L1_ffn_rb1"])], 0).T)
    b2 = np.ascontiguousarray(np.concatenate(
        [np.asarray(inputs["L1_ffn_rb2"]), np.asarray(inputs["L1_ffn_sb2"])], 0))
    wat = np.asarray(inputs["L1_attn_w_attn"])
    bat = np.asarray(inputs["L1_attn_b_attn"])

    maps = []
    for core in range(8):
        b_, h_ = core // 2, core % 2
        lo = h_ * T
        sl = x[b_, lo:min(4096, lo + TH)]
        if sl.shape[0] < TH:
            sl = np.concatenate([sl, np.zeros((TH - sl.shape[0], D), np.float32)], 0)
        pr = h_
        m = {
            "xin": np.ascontiguousarray(sl),
            "wg": np.ascontiguousarray(np.asarray(inputs["L0_ffn_w_gate"])),
            "wu": np.ascontiguousarray(np.asarray(inputs["L0_ffn_w_up"])),
            "wdn": np.ascontiguousarray(np.asarray(inputs["L0_ffn_w_down"])),
            "wcT": np.ascontiguousarray(np.asarray(inputs["L0_attn_W"]).T),
            "bconv": np.ascontiguousarray(np.asarray(inputs["L0_attn_b"]).reshape(32, 1)),
            "bcr": np.ascontiguousarray(np.asarray(inputs["L0_attn_b"]).reshape(1, 32)),
            "wqT": np.ascontiguousarray(wat[pr * 512:(pr + 1) * 512, :].T),
            "wkT": np.ascontiguousarray(wat[1024 + pr * 512:1024 + (pr + 1) * 512, :].T),
            "wvT": np.ascontiguousarray(wat[2048 + pr * 512:2048 + (pr + 1) * 512, :].T),
            "bqd": np.ascontiguousarray(bat[pr * 512:(pr + 1) * 512].reshape(4, 128).T),
            "bkd": np.ascontiguousarray(bat[1024 + pr * 512:1024 + (pr + 1) * 512].reshape(4, 128).T),
            "bvd": np.ascontiguousarray(bat[2048 + pr * 512:2048 + (pr + 1) * 512].reshape(4, 128).T),
            "wpT": np.ascontiguousarray(np.asarray(inputs["L1_attn_w_proj"]).T[pr * 512:(pr + 1) * 512, :]),
            "bpd": np.ascontiguousarray(np.asarray(inputs["L1_attn_b_proj"]).reshape(8, 128).T),
            "gwd": np.ascontiguousarray(np.asarray(inputs["L1_ffn_gate_w"])),
            "gbd": np.ascontiguousarray(np.asarray(inputs["L1_ffn_gate_b"]).reshape(1, 8)),
            "w1d": w1, "w2d": w2, "b1Td": b1T, "b2d": b2,
            "idfd": idf, "idbd": idb, "shmd": shm, "amd": am,
        }
        maps.append(m)
    return maps


def kernel(**inputs):
    maps = _stage(inputs)
    nc = build(STAGE)
    with allow_ici_all_to_all():
        res = run_bass_kernel_spmd(nc, maps, core_ids=list(range(8)))
    out = np.zeros((4, 4096, D), np.float32)
    for core in range(8):
        b_, h_ = core // 2, core % 2
        out[b_, h_ * T:(h_ + 1) * T] = res.results[core]["xout"]
    return out
